# version 04 (final)
# baseline (speedup 1.0000x reference)
import numpy as np
import concourse.bass as bass

DEBUG_TAPS = False
import concourse.mybir as mybir
from concourse.bass_utils import run_bass_kernel_spmd

F32 = mybir.dt.float32
AF = mybir.ActivationFunctionType
ALU = mybir.AluOpType

H = 4096
S = 128
NH = 64
EPS = 1e-5
CONV = H + 2 * S
NJC_A = 35
R = 1024
TQ = 256
NQU = 4
NCH = 8
NDC = 32
BLK = 128 * 512

OFF_A = 0
OFF_Z = OFF_A + 9 * 32
OFF_O = OFF_Z + 8 * 32
OFF_F1 = OFF_O + 8 * 32
OFF_F2 = OFF_F1 + 8 * 4 * 32
NBLK = OFF_F2 + 8 * 8 * 16
W_ELEMS = NBLK * BLK

C_ID, C_LT, C_CONV, C_MUP, C_DBC, C_B1, C_B2, C_SEL, C_M, C_AL, C_DTB, C_A = (
    0, 128, 256, 396, 463, 527, 655, 687, 691, 692, 693, 694)
CST_W = 696

AR_SZ = 32896
A_XQ = 0
A_XTOK = 0
A_UT = 8192
A_CSROW = 16512
A_LNW = 24704
B_XQ = 0
B_UT = 4096
B_SZQ = 12416
B_YHT = 20608
F_U2T = 0
F_O2FM = 16384


def _host_weight_stream(W_in, W_out, W1, W2):
    Wp = np.zeros((H, H + 9 * 512), dtype=np.float32)
    Wp[:, :8512] = W_in
    blocks = []
    for jb in range(9):
        for dc in range(NDC):
            c0 = H + jb * 512
            blocks.append(Wp[dc * 128:(dc + 1) * 128, c0:c0 + 512])
    for zb in range(8):
        for dc in range(NDC):
            blocks.append(Wp[dc * 128:(dc + 1) * 128, zb * 512:(zb + 1) * 512])
    for ob in range(8):
        for fc in range(NDC):
            blocks.append(W_out[fc * 128:(fc + 1) * 128, ob * 512:(ob + 1) * 512])
    for fb in range(8):
        for fg in range(4):
            for dc in range(NDC):
                c0 = fb * 2048 + fg * 512
                blocks.append(W1[dc * 128:(dc + 1) * 128, c0:c0 + 512])
    for fb in range(8):
        for og in range(8):
            for fc16 in range(16):
                r0 = (fb * 16 + fc16) * 128
                blocks.append(W2[r0:r0 + 128, og * 512:(og + 1) * 512])
    assert len(blocks) == NBLK
    return np.concatenate([np.ascontiguousarray(b).reshape(-1) for b in blocks])


def build_program():
    nc = bass.Bass("TRN2", target_bir_lowering=False)

    x_in = nc.dram_tensor("x", [R, H], F32, kind="ExternalInput").ap()
    halo_in = nc.dram_tensor("halo", [4, H], F32, kind="ExternalInput").ap()
    wshp = [nc.dram_tensor(f"wsh{p}", [(NBLK // 4) * BLK // 8], F32,
                           kind="ExternalInput").ap() for p in range(4)]
    lnw_in = nc.dram_tensor("lnw_bc", [128, H], F32, kind="ExternalInput").ap()
    mupz_in = nc.dram_tensor("mupz_bc", [128, H], F32, kind="ExternalInput").ap()
    nrmw_in = nc.dram_tensor("normw_bc", [128, H], F32, kind="ExternalInput").ap()
    cst_in = nc.dram_tensor("cst_in", [128, CST_W], F32, kind="ExternalInput").ap()
    out_t = nc.dram_tensor("out", [R, H], F32, kind="ExternalOutput").ap()

    w_shin = [nc.dram_tensor(f"w_shin{p}", [(NBLK // 4) * BLK // 8], F32)
              for p in range(4)]
    w_allp = [nc.dram_tensor(f"w_all{p}", [(NBLK // 4) * BLK], F32)
              for p in range(4)]
    y_loc = nc.dram_tensor("y_loc", [R, H], F32)
    x2_loc = nc.dram_tensor("x2_loc", [R, H], F32)
    st_in = nc.dram_tensor("st_in", [128 * 4097], F32)
    csrow_d = nc.dram_tensor("csrow_d", [8192], F32)
    w4_d = nc.dram_tensor("w4_d", [256], F32)
    offs_d = nc.dram_tensor("offs_d", [9 * 64], F32)
    st_out = nc.dram_tensor("st_out", [4 * 128 * 4097], F32)
    if DEBUG_TAPS:
        dbg_ut = nc.dram_tensor("dbg_ut", [128, 8320], F32)
        dbg_zr = nc.dram_tensor("dbg_zr", [128, 512], F32)
        dbg_sz = nc.dram_tensor("dbg_sz", [128, 8192], F32)
        dbg_y0 = nc.dram_tensor("dbg_y0", [128, H], F32)
        dbg_y1 = nc.dram_tensor("dbg_y1", [128, H], F32)
        dbg_y2 = nc.dram_tensor("dbg_y2", [128, H], F32)
        dbg_yh = nc.dram_tensor("dbg_yh", [128, H], F32)

    ctx = []

    def sbuf(name, shape, dtype=F32):
        t = nc.sbuf_tensor(name, shape, dtype)
        ctx.append(t)
        return t.__enter__()

    def psum(name):
        t = nc.psum_tensor(name, [128, 512], F32)
        ctx.append(t)
        return t.__enter__()

    AR = sbuf("AR", [128, AR_SZ])
    wb = sbuf("wb", [128, 3 * 512])
    HH = sbuf("HH", [128, 8192])
    ysb = sbuf("ysb", [128, H])
    cst = sbuf("cst", [128, CST_W])
    ones1 = sbuf("ones1", [1, 128])
    ons = sbuf("ons", [128, 128])
    uTh = sbuf("uTh", [128, 32 * 4])
    xbcdt = sbuf("xbcdt", [128, 264])
    convt = sbuf("convt", [128, 256])
    xbcsj = sbuf("xbcsj", [128, 2 * 256])
    hk = sbuf("hk", [128, NJC_A * 3])
    dtq = sbuf("dtq", [128, 256])
    ldaq = sbuf("ldaq", [128, 256])
    csb = sbuf("csb", [128, 128])
    offs = sbuf("offs", [128, 16])
    csave = AR[:, 28800:28800 + NCH * 128]
    comboT = AR[:, 29824:29824 + NCH * 64]
    bkeep = AR[:, 30336:30336 + 256]
    dtT = AR[:, 30592:30592 + 64]
    cslast = AR[0:1, 30656:30656 + 64]
    offsrow = AR[0:1, 30720:30720 + 9 * 64]
    cbt = sbuf("cbt", [128, 128])
    btok = sbuf("btok", [128, 128])
    etm = sbuf("etm", [128, 128])
    esb = sbuf("esb", [128, 128])
    w1m = sbuf("w1m", [128, 128])
    gbc = sbuf("gbc", [128, 64])
    ecsT = sbuf("ecsT", [128, 64])
    wE = sbuf("wE", [128, 64])
    t64 = sbuf("t64", [128, 64])
    xw = sbuf("xw", [128, 64])
    stat = sbuf("stat", [128, 64])
    mval = sbuf("mval", [128, 1])
    gval = sbuf("gval", [128, 1])
    t0v = sbuf("t0v", [128, 1])
    t1v = sbuf("t1v", [128, 1])
    lam4 = sbuf("lam4", [4, 64])
    w4 = sbuf("w4", [4, 64])
    w4row = sbuf("w4row", [1, 256])
    ztmp = sbuf("ztmp", [128, 2 * 512])
    x2s = ztmp
    auxb = sbuf("auxb", [128, 2 * 512])

    P = [psum(f"P{i}") for i in range(8)]
    PT = P[6]
    PBC = P[7]

    ident = cst[:, C_ID:C_ID + 128]
    ltm = cst[:, C_LT:C_LT + 128]

    ENGS = ("sync", "gpsimd", "tensor", "vector", "scalar")
    OPS = {e: [] for e in ENGS}
    CNT = {k: 0 for k in ("s_w0", "s_w1", "s_w2", "s_io", "s_mm",
                          "s_dve", "s_act", "s_cc", "s_st", "s_ys", "s_x2")}
    sems = {}

    def emit(eng, fn):
        OPS[eng].append(fn)

    def wait(eng, sem, val):
        if val <= 0:
            return
        emit(eng, lambda e, sem=sem, val=val: e.wait_ge(sems[sem], val))

    def wait_io(eng):
        wait(eng, "s_io", CNT["s_io"])

    SELF_SEM = {"vector": "s_dve", "scalar": "s_act"}

    def op(eng, fn, incs=()):
        incs = list(incs)
        own = SELF_SEM.get(eng)
        if own is not None:
            if CNT[own] > 0:
                wait(eng, own, CNT[own])
            if not any(s_ == own for s_, _ in incs):
                incs.append((own, 1))
        for s_, n_ in incs:
            CNT[s_] += n_
        after = {s_: CNT[s_] for s_, _ in incs}

        assert len(incs) <= 1, incs

        def run(e, fn=fn, incs=incs):
            i = fn(e)
            if incs:
                i.then_inc(sems[incs[0][0]], incs[0][1])
        emit(eng, run)
        return after

    wstate = {"i": 0}
    slot_hist = {}

    def w_load(blk_id):
        i = wstate["i"]
        slot = i % 3
        if slot in slot_hist:
            wait("sync", "s_mm", slot_hist[slot])
        part, boff = divmod(blk_id, NBLK // 4)
        src = w_allp[part].ap()[boff * BLK:(boff + 1) * BLK].rearrange(
            "(p f) -> p f", p=128)
        c = op("sync", lambda e, slot=slot, src=src: e.dma_start(
            out=wb[:, slot * 512:(slot + 1) * 512], in_=src),
            [(f"s_w{slot}", 16)])
        wstate["i"] += 1
        return slot, c[f"s_w{slot}"]

    def w_done(slot):
        slot_hist[slot] = CNT["s_mm"]

    def mm(out, lhsT, rhs, start, stop, incs=()):
        return op("tensor", lambda e, out=out, lhsT=lhsT, rhs=rhs, start=start,
                  stop=stop: e.matmul(out, lhsT, rhs, start=start, stop=stop),
                  incs)

    def trans(out, in_, idap, incs=()):
        return op("tensor", lambda e, out=out, in_=in_, idap=idap:
                  e.transpose(out, in_, idap), incs)

    bank_free = {}

    def bank_wait(p):
        if id(p) in bank_free:
            sem, val = bank_free[id(p)]
            wait("tensor", sem, val)

    def bank_set(p, sem, val):
        bank_free[id(p)] = (sem, val)

    def dma(out, in_, sem="s_io", slow=False):
        wait("gpsimd", sem, CNT[sem])
        c = op("gpsimd", lambda e, out=out, in_=in_, slow=slow:
               e.dma_start(out=out, in_=in_,
                           allow_slow_non_contiguous=slow), [(sem, 16)])
        return c[sem]

    def quiesce_io():
        wait("gpsimd", "s_mm", CNT["s_mm"])
        wait("gpsimd", "s_dve", CNT["s_dve"])
        wait("gpsimd", "s_act", CNT["s_act"])

    io_cst = dma(cst[:, 0:694], cst_in[:, 0:694])
    cc0 = 0
    for p in range(4):
        cc0 = dma(w_shin[p].ap(), wshp[p])
    wait_io("gpsimd")
    cw = {"s_cc": cc0}
    for p in range(4):
        cw = op("gpsimd", lambda e, p=p: e.collective_compute(
            "AllGather", ALU.bypass, replica_groups=[list(range(8))],
            ins=[w_shin[p].ap()], outs=[w_allp[p].ap()]), [("s_cc", 1)])
    wait("sync", "s_cc", cw["s_cc"])
    io_lnw = dma(AR[:, A_LNW:A_LNW + H], lnw_in)
    io_halo = dma(AR[0:4, A_XQ:A_XQ + H], halo_in)

    wait_io("scalar")
    a0 = op("scalar", lambda e: e.activation(
        out=cst[0:64, C_A:C_A + 1], in_=cst[0:64, C_AL:C_AL + 1], func=AF.Exp),
        [("s_act", 1)])["s_act"]
    wait("vector", "s_act", a0)
    op("vector", lambda e: e.tensor_scalar(
        cst[0:64, C_A:C_A + 1], cst[0:64, C_A:C_A + 1], -1.0, None, ALU.mult))
    op("vector", lambda e: e.memset(ones1[:, :], 1.0))
    op("vector", lambda e: e.memset(ons[:, :], 1.0))
    op("vector", lambda e: e.memset(offs[:, :], 0.0))
    d0 = op("vector", lambda e: e.memset(HH[:, 0:4096], 0.0),
            [("s_dve", 1)])["s_dve"]

    def stats_and_u(xap, uap, npart, double_norm, lnw_stream=False):
        for k in range(8):
            op("vector", lambda e, k=k, xap=xap, npart=npart: e.bn_stats(
                stat[0:npart, 6 * k:6 * k + 6], xap[:, 512 * k:512 * (k + 1)]),
                [("s_dve", 1)])
        wait("vector", "s_dve", CNT["s_dve"])
        op("vector", lambda e, npart=npart: e.bn_aggr(
            stat[0:npart, 48:50], stat[0:npart, 0:48]), [("s_dve", 1)])
        wait("vector", "s_dve", CNT["s_dve"])
        op("vector", lambda e, npart=npart: e.tensor_copy(
            mval[0:npart, :], stat[0:npart, 48:49]))
        d = op("vector", lambda e, npart=npart: e.tensor_scalar(
            t0v[0:npart, :], stat[0:npart, 49:50], EPS, None, ALU.add),
            [("s_dve", 1)])["s_dve"]
        wait("scalar", "s_dve", d)
        a = op("scalar", lambda e, npart=npart: e.activation(
            out=t1v[0:npart, :], in_=t0v[0:npart, :], func=AF.Ln),
            [("s_act", 1)])["s_act"]
        if double_norm:
            wait("scalar", "s_act", a)
            a = op("scalar", lambda e, npart=npart: e.activation(
                out=t0v[0:npart, :], in_=t1v[0:npart, :], func=AF.Exp,
                scale=-1.0), [("s_act", 1)])["s_act"]
            wait("vector", "s_act", a)
            op("vector", lambda e, npart=npart: e.scalar_tensor_tensor(
                gval[0:npart, :], stat[0:npart, 49:50], 1.0, t0v[0:npart, :],
                ALU.mult, ALU.mult))
            d = op("vector", lambda e, npart=npart: e.tensor_scalar(
                gval[0:npart, :], gval[0:npart, :], EPS, None, ALU.add),
                [("s_dve", 1)])["s_dve"]
            wait("scalar", "s_dve", d)
            a = op("scalar", lambda e, npart=npart: e.activation(
                out=gval[0:npart, :], in_=gval[0:npart, :], func=AF.Ln),
                [("s_act", 1)])["s_act"]
            wait("vector", "s_act", a)
            d = op("vector", lambda e, npart=npart: e.tensor_tensor(
                gval[0:npart, :], gval[0:npart, :], t1v[0:npart, :], ALU.add),
                [("s_dve", 1)])["s_dve"]
            wait("scalar", "s_dve", d)
            a = op("scalar", lambda e, npart=npart: e.activation(
                out=gval[0:npart, :], in_=gval[0:npart, :], func=AF.Exp,
                scale=-0.5), [("s_act", 1)])["s_act"]
        else:
            wait("scalar", "s_act", a)
            a = op("scalar", lambda e, npart=npart: e.activation(
                out=gval[0:npart, :], in_=t1v[0:npart, :], func=AF.Exp,
                scale=-0.5), [("s_act", 1)])["s_act"]
        wait("vector", "s_act", a)
        op("vector", lambda e, xap=xap, uap=uap, npart=npart: e.tensor_scalar(
            uap, xap, mval[0:npart, :], gval[0:npart, :],
            ALU.subtract, ALU.mult))
        if not double_norm:
            d = op("vector", lambda e, uap=uap: e.tensor_scalar(
                uap, uap, 1.0, None, ALU.mult), [("s_dve", 1)])["s_dve"]
        elif lnw_stream:
            for sl8 in range(8):
                slot = sl8 % 2
                if aux_free[slot]:
                    wait("gpsimd", "s_dve", aux_free[slot])
                io_l = dma(auxb[:, slot * 512:slot * 512 + 512],
                           lnw_in[:, sl8 * 512:(sl8 + 1) * 512])
                wait_io("vector")
                d = op("vector", lambda e, uap=uap, sl8=sl8, slot=slot:
                       e.tensor_tensor(
                           uap[:, sl8 * 512:(sl8 + 1) * 512],
                           uap[:, sl8 * 512:(sl8 + 1) * 512],
                           auxb[:, slot * 512:slot * 512 + 512], ALU.mult),
                       [("s_dve", 1)])["s_dve"]
                aux_free[slot] = d
        else:
            d = op("vector", lambda e, uap=uap, npart=npart: e.tensor_tensor(
                uap, uap, AR[0:npart, A_LNW:A_LNW + H], ALU.mult),
                [("s_dve", 1)])["s_dve"]
        return d

    wait_io("vector")
    wait_io("vector")
    wait("vector", "s_dve", d0)
    d_uh = stats_and_u(AR[0:4, A_XQ:A_XQ + H], AR[0:4, A_XQ:A_XQ + H], 4, True)
    wait("tensor", "s_dve", d_uh)
    for dc in range(NDC):
        bank_wait(PT)
        m = trans(PT[0:128, 0:4], AR[0:4, A_XQ + dc * 128:A_XQ + (dc + 1) * 128],
                  cst[0:4, C_ID:C_ID + 4], [("s_mm", 1)])["s_mm"]
        wait("vector", "s_mm", m)
        d = op("vector", lambda e, dc=dc: e.tensor_copy(
            uTh[:, dc * 4:(dc + 1) * 4], PT[0:128, 0:4]),
            [("s_dve", 1)])["s_dve"]
        bank_set(PT, "s_dve", d)

    y_store = [0]
    io_cr_last = [0]
    aux_free = {0: 0, 1: 0}
    silu_free = {0: ("s_act", 0), 1: ("s_act", 0)}
    prev_silu_act = [0]

    for q in range(NQU):
        first_q = (q == 0)
        NTOK = 260 if first_q else 256
        for tt in range(2):
            quiesce_io()
            io_x = dma(AR[:, A_XQ:A_XQ + H],
                       x_in[q * TQ + tt * 128:q * TQ + (tt + 1) * 128, :])
            wait_io("vector")
            d_u = stats_and_u(AR[:, A_XQ:A_XQ + H], AR[:, A_XQ:A_XQ + H],
                              128, True)
            wait("tensor", "s_dve", d_u)
            for dc in range(NDC):
                bank_wait(PT)
                m = trans(PT[:, 0:128],
                          AR[:, A_XQ + dc * 128:A_XQ + (dc + 1) * 128],
                          ident, [("s_mm", 1)])["s_mm"]
                wait("vector", "s_mm", m)
                d = op("vector", lambda e, dc=dc, tt=tt: e.tensor_copy(
                    AR[:, A_UT + dc * 260 + tt * 128:
                       A_UT + dc * 260 + (tt + 1) * 128],
                    PT[:, 0:128]), [("s_dve", 1)])["s_dve"]
                bank_set(PT, "s_dve", d)
        if first_q:
            for dc in range(NDC):
                op("vector", lambda e, dc=dc: e.tensor_copy(
                    AR[:, A_UT + dc * 260 + 256:A_UT + dc * 260 + 260],
                    uTh[:, dc * 4:(dc + 1) * 4]),
                   [("s_dve", 1)] if dc == NDC - 1 else ())
        d_uT = CNT["s_dve"]

        ch_base = q * 2
        for jb in range(9):
            njc = 4 if jb < 8 else 3
            for dc in range(NDC):
                slot, wcnt = w_load(OFF_A + jb * 32 + dc)
                wait("tensor", f"s_w{slot}", wcnt)
                if dc == 0:
                    wait("tensor", "s_dve", d_uT)
                    for j in range(njc):
                        bank_wait(P[j])
                for j in range(njc):
                    mm(P[j][:, 0:NTOK],
                       wb[:, slot * 512 + j * 128:slot * 512 + (j + 1) * 128],
                       AR[:, A_UT + dc * 260:A_UT + dc * 260 + NTOK],
                       start=(dc == 0), stop=(dc == NDC - 1),
                       incs=[("s_mm", 1)])
                w_done(slot)
            m_last = CNT["s_mm"]
            for j in range(njc):
                jc = jb * 4 + j
                if jc >= NJC_A:
                    continue
                rot = jc % 2
                buf = xbcdt[:, 0:264]
                wait("vector", "s_mm", m_last - njc + j + 1)
                d = op("vector", lambda e, j=j, buf=buf, jc=jc, NTOK=NTOK:
                       e.tensor_scalar(buf[:, 3:3 + NTOK], P[j][:, 0:NTOK],
                                       cst[:, C_MUP + 32 + jc:C_MUP + 33 + jc],
                                       None, ALU.mult), [("s_dve", 1)])["s_dve"]
                bank_set(P[j], "s_dve", d)
                if jc < 34:
                    if first_q:
                        op("vector", lambda e, buf=buf: e.tensor_copy(
                            buf[:, 0:3], buf[:, 260:263]))
                    else:
                        op("vector", lambda e, buf=buf, jc=jc: e.tensor_copy(
                            buf[:, 0:3], hk[:, jc * 3:(jc + 1) * 3]))
                    op("vector", lambda e, buf=buf, jc=jc: e.tensor_copy(
                        hk[:, jc * 3:(jc + 1) * 3], buf[:, 256:259]))
                    wait("vector", "s_act", prev_silu_act[0])
                    op("vector", lambda e, buf=buf, jc=jc: e.tensor_scalar(
                        convt[:, :], buf[:, 0:256],
                        cst[:, C_CONV + jc * 4:C_CONV + jc * 4 + 1], None,
                        ALU.mult))
                    for k in range(1, 4):
                        inc = [("s_dve", 1)] if k == 3 else ()
                        op("vector", lambda e, buf=buf, jc=jc, k=k:
                           e.scalar_tensor_tensor(
                               convt[:, :], buf[:, k:k + 256],
                               cst[:, C_CONV + jc * 4 + k:
                                    C_CONV + jc * 4 + k + 1],
                               convt[:, :], ALU.mult, ALU.add), inc)
                    dcv = CNT["s_dve"]
                    wait("scalar", "s_dve", dcv)
                    sem_f, val_f = silu_free[rot]
                    if val_f > 0:
                        wait("scalar", sem_f, val_f)
                    a = op("scalar", lambda e, rot=rot: e.activation(
                        out=xbcsj[:, rot * 256:rot * 256 + 256],
                        in_=convt[:, :], func=AF.Silu),
                        [("s_act", 1)])["s_act"]
                    prev_silu_act[0] = a
                    if jc < 32:
                        wait("tensor", "s_act", a)
                        last_m = 0
                        for cc_ in range(2):
                            bank_wait(PT)
                            m = trans(PT[:, 0:128],
                                      xbcsj[:, rot * 256 + cc_ * 128:
                                            rot * 256 + (cc_ + 1) * 128],
                                      ident, [("s_mm", 1)])["s_mm"]
                            last_m = m
                            wait("vector", "s_mm", m)
                            d = op("vector", lambda e, jc=jc, cc_=cc_:
                                   e.tensor_copy(
                                       AR[:, A_XTOK + cc_ * 4096 + jc * 128:
                                          A_XTOK + cc_ * 4096 + (jc + 1) * 128],
                                       PT[:, 0:128]), [("s_dve", 1)])["s_dve"]
                            bank_set(PT, "s_dve", d)
                        silu_free[rot] = ("s_mm", last_m)
                    elif jc == 32:
                        wait("vector", "s_act", a)
                        d = op("vector", lambda e, rot=rot: e.tensor_copy(
                            bkeep[:, :], xbcsj[:, rot * 256:rot * 256 + 256]),
                            [("s_dve", 1)])["s_dve"]
                        silu_free[rot] = ("s_dve", d)
                    else:
                        wait("vector", "s_act", a)
                        d = op("vector", lambda e, rot=rot, ch_base=ch_base:
                               e.tensor_copy(
                                   csave[:, ch_base * 128:(ch_base + 2) * 128],
                                   xbcsj[:, rot * 256:rot * 256 + 256]),
                               [("s_dve", 1)])["s_dve"]
                        silu_free[rot] = ("s_dve", d)
                else:
                    d = op("vector", lambda e, buf=buf: e.tensor_copy(
                        dtq[0:64, :], buf[0:64, 3:259]),
                        [("s_dve", 1)])["s_dve"]
                    wait("scalar", "s_dve", d)
                    a = op("scalar", lambda e: e.activation(
                        out=ldaq[0:64, :], in_=dtq[0:64, :], func=AF.Exp,
                        bias=cst[0:64, C_DTB:C_DTB + 1]),
                        [("s_act", 1)])["s_act"]
                    wait("vector", "s_act", a)
                    d = op("vector", lambda e: e.tensor_scalar(
                        ldaq[0:64, :], ldaq[0:64, :], 1.0, None, ALU.add),
                        [("s_dve", 1)])["s_dve"]
                    wait("scalar", "s_dve", d)
                    a = op("scalar", lambda e: e.activation(
                        out=dtq[0:64, :], in_=ldaq[0:64, :], func=AF.Ln),
                        [("s_act", 1)])["s_act"]
                    wait("vector", "s_act", a)
                    op("vector", lambda e: e.tensor_scalar(
                        ldaq[0:64, :], dtq[0:64, :],
                        cst[0:64, C_A:C_A + 1], None, ALU.mult))

        for cc_ in range(2):
            ch = ch_base + cc_
            tsl = slice(cc_ * 128, (cc_ + 1) * 128)
            wait_io("vector")
            d_cs = op("vector", lambda e, tsl=tsl: e.tensor_tensor_scan(
                csb[0:64, :], ons[0:64, :], ldaq[0:64, tsl], 0.0,
                ALU.mult, ALU.add), [("s_dve", 1)])["s_dve"]
            op("vector", lambda e, ch=ch: e.tensor_tensor(
                offs[0:64, ch + 1:ch + 2], offs[0:64, ch:ch + 1],
                csb[0:64, 127:128], ALU.add))
            wait("gpsimd", "s_dve", d_cs)
            io_cr0 = dma(csrow_d.ap().rearrange("(p f) -> p f", p=64),
                         csb[0:64, :])
            wait_io("gpsimd")
            io_cr = dma(AR[0:1, A_CSROW:A_CSROW + 8192],
                        csrow_d.ap().rearrange("(p f) -> p f", p=1))
            io_cl = dma(cslast,
                        csrow_d.ap().rearrange("(a b) -> b a", b=128)[127:128, :],
                        slow=True)
            io_cr_last[0] = io_cl
            wait("tensor", "s_dve", d_cs)
            bank_wait(PT)
            m = trans(PT[:, 0:64], csb[0:64, :], cst[0:64, C_ID:C_ID + 64],
                      [("s_mm", 1)])["s_mm"]
            wait("vector", "s_mm", m)
            d = op("vector", lambda e, ch=ch: e.tensor_copy(
                comboT[:, ch * 64:(ch + 1) * 64], PT[:, 0:64]),
                [("s_dve", 1)])["s_dve"]
            bank_set(PT, "s_dve", d)
            bank_wait(PT)
            m = trans(PT[:, 0:64], dtq[0:64, tsl], cst[0:64, C_ID:C_ID + 64],
                      [("s_mm", 1)])["s_mm"]
            wait("vector", "s_mm", m)
            d = op("vector", lambda e: e.tensor_copy(dtT[:, :], PT[:, 0:64]),
                   [("s_dve", 1)])["s_dve"]
            bank_set(PT, "s_dve", d)
            bank_wait(PT)
            m = trans(PT[:, 0:128], bkeep[:, tsl], ident,
                      [("s_mm", 1)])["s_mm"]
            wait("vector", "s_mm", m)
            d = op("vector", lambda e: e.tensor_copy(btok[:, :], PT[:, 0:128]),
                   [("s_dve", 1)])["s_dve"]
            bank_set(PT, "s_dve", d)
            bank_wait(PBC)
            m = mm(PBC[:, 0:128], bkeep[:, tsl],
                   csave[:, ch * 128:(ch + 1) * 128], True, True,
                   [("s_mm", 1)])["s_mm"]
            wait("vector", "s_mm", m)
            d = op("vector", lambda e: e.tensor_copy(cbt[:, :], PBC[:, 0:128]),
                   [("s_dve", 1)])["s_dve"]
            bank_set(PBC, "s_dve", d)
            wait_io("tensor")
            bank_wait(P[4])
            m = mm(P[4][:, 0:64], ones1[0:1, 0:128], cslast[0:1, :],
                   True, True, [("s_mm", 1)])["s_mm"]
            wait("scalar", "s_mm", m)
            a = op("scalar", lambda e: e.activation(
                out=gbc[:, :], in_=P[4][:, 0:64], func=AF.Exp),
                [("s_act", 1)])["s_act"]
            bank_set(P[4], "s_act", a)
            wait("scalar", "s_dve", CNT["s_dve"])
            a = op("scalar", lambda e, ch=ch: e.activation(
                out=ecsT[:, :], in_=comboT[:, ch * 64:(ch + 1) * 64],
                func=AF.Exp), [("s_act", 1)])["s_act"]
            wait("vector", "s_act", a)
            wait("vector", "s_ys", CNT["s_ys"])

            for h in range(NH):
                pr = h % 2
                pE = P[0 + pr]
                pYI = P[2 + pr]
                pYX = P[4 + pr]
                pDH = PT if pr else PBC
                xsl = AR[:, A_XTOK + cc_ * 4096 + h * 64:
                         A_XTOK + cc_ * 4096 + (h + 1) * 64]
                bank_wait(pE)
                m = mm(pE[:, 0:128], ones1[0:1, 0:128],
                       AR[0:1, A_CSROW + h * 128:A_CSROW + (h + 1) * 128],
                       True, True, [("s_mm", 1)])["s_mm"]
                wait("vector", "s_mm", m)
                d = op("vector", lambda e, pE=pE, ch=ch, h=h: e.tensor_scalar(
                    etm[:, :], pE[:, 0:128],
                    comboT[:, ch * 64 + h:ch * 64 + h + 1], 0.0,
                    ALU.subtract, ALU.min), [("s_dve", 1)])["s_dve"]
                bank_set(pE, "s_dve", d)
                wait("scalar", "s_dve", d)
                a = op("scalar", lambda e: e.activation(
                    out=esb[:, :], in_=etm[:, :], func=AF.Exp),
                    [("s_act", 1)])["s_act"]
                wait("vector", "s_act", a)
                op("vector", lambda e, h=h: e.scalar_tensor_tensor(
                    w1m[:, :], esb[:, :], dtT[:, h:h + 1], ltm,
                    ALU.mult, ALU.mult))
                op("vector", lambda e, xsl=xsl: e.tensor_scalar(
                    xw[:, :], xsl, w1m[:, 127:128], None, ALU.mult))
                d_w = op("vector", lambda e: e.tensor_tensor(
                    w1m[:, :], w1m[:, :], cbt[:, :], ALU.mult),
                    [("s_dve", 1)])["s_dve"]
                wait("tensor", "s_dve", d_w)
                bank_wait(pYI)
                mYI = mm(pYI[:, 0:64], w1m[:, :], xsl, True, True,
                         [("s_mm", 1)])["s_mm"]
                bank_wait(pYX)
                mYX = mm(pYX[:, 0:64], csave[:, ch * 128:(ch + 1) * 128],
                         HH[:, h * 64:(h + 1) * 64], True, True,
                         [("s_mm", 1)])["s_mm"]
                bank_wait(pDH)
                mDH = mm(pDH[:, 0:64], btok[:, :], xw[:, :], True, True,
                         [("s_mm", 1)])["s_mm"]
                wait("vector", "s_mm", mYX)
                d = op("vector", lambda e, pYX=pYX, h=h: e.tensor_scalar(
                    t64[:, :], pYX[:, 0:64], ecsT[:, h:h + 1], None, ALU.mult),
                    [("s_dve", 1)])["s_dve"]
                bank_set(pYX, "s_dve", d)
                wait("vector", "s_mm", mYI)
                d = op("vector", lambda e, pYI=pYI: e.tensor_tensor(
                    t64[:, :], t64[:, :], pYI[:, 0:64], ALU.add),
                    [("s_dve", 1)])["s_dve"]
                bank_set(pYI, "s_dve", d)
                op("vector", lambda e, h=h, xsl=xsl: e.scalar_tensor_tensor(
                    ysb[:, h * 64:(h + 1) * 64], xsl,
                    cst[:, C_DBC + h:C_DBC + h + 1], t64[:, :],
                    ALU.mult, ALU.add))
                wait("vector", "s_mm", mDH)
                d = op("vector", lambda e, pDH=pDH, h=h: e.scalar_tensor_tensor(
                    HH[:, h * 64:(h + 1) * 64], HH[:, h * 64:(h + 1) * 64],
                    gbc[:, h:h + 1], pDH[:, 0:64], ALU.mult, ALU.add),
                    [("s_dve", 1)])["s_dve"]
                bank_set(pDH, "s_dve", d)
            d_y = CNT["s_dve"]
            wait("gpsimd", "s_dve", d_y)
            y_store[0] = dma(y_loc.ap()[ch * 128:(ch + 1) * 128, :], ysb[:, :],
                             sem="s_ys")

    d_all = CNT["s_dve"]
    wait("gpsimd", "s_dve", d_all)
    io_od = dma(offs_d.ap().rearrange("(f p) -> p f", p=64), offs[0:64, 0:9],
                slow=True)
    wait_io("gpsimd")
    dma(offsrow, offs_d.ap().rearrange("(p f) -> p f", p=1))
    io_st1 = dma(st_in.ap()[0:128 * 4096].rearrange("(p f) -> p f", p=128),
                 HH[:, 0:4096])
    io_st2 = dma(st_in.ap()[128 * 4096:128 * 4096 + 64].rearrange(
                     "(p f) -> p f", p=64),
                 offs[0:64, 8:9])
    dma(st_in.ap()[128 * 4096 + 64:128 * 4097].rearrange("(p f) -> p f", p=64),
        offs[64:128, 0:1])
    wait_io("gpsimd")
    ccs = op("gpsimd", lambda e: e.collective_compute(
        "AllGather", ALU.bypass, replica_groups=[[0, 1, 2, 3], [4, 5, 6, 7]],
        ins=[st_in.ap()], outs=[st_out.ap()]), [("s_cc", 1)])["s_cc"]
    wait("gpsimd", "s_cc", ccs)
    io_lam = 0
    for j in range(4):
        io_lam = dma(lam4[j:j + 1, 0:64],
                     st_out.ap()[j * 128 * 4097 + 128 * 4096:
                                 j * 128 * 4097 + 128 * 4096 + 64]
                     .rearrange("(p f) -> p f", p=1))
    wait_io("tensor")
    bank_wait(P[0])
    m = mm(P[0][0:4, 0:64], cst[0:4, C_SEL:C_SEL + 4], lam4[0:4, 0:64],
           True, True, [("s_mm", 1)])["s_mm"]
    wait("scalar", "s_mm", m)
    a = op("scalar", lambda e: e.activation(
        out=w4[0:4, 0:64], in_=P[0][0:4, 0:64], func=AF.Exp),
        [("s_act", 1)])["s_act"]
    bank_set(P[0], "s_act", a)
    wait("vector", "s_act", a)
    d_w4v = op("vector", lambda e: e.tensor_scalar(
        w4[0:4, 0:64], w4[0:4, 0:64], cst[0:4, C_M:C_M + 1], None, ALU.mult),
        [("s_dve", 1)])["s_dve"]
    wait("gpsimd", "s_dve", d_w4v)
    io_w4a = dma(w4_d.ap().rearrange("(p f) -> p f", p=4), w4[0:4, 0:64])
    wait_io("gpsimd")
    io_w4 = dma(w4row[0:1, :], w4_d.ap().rearrange("(p f) -> p f", p=1))
    d_prev_e = 0
    for j in range(4):
        if d_prev_e:
            wait("gpsimd", "s_dve", d_prev_e)
        io_e = dma(AR[:, B_SZQ:B_SZQ + 4096],
                   st_out.ap()[j * 128 * 4097:j * 128 * 4097 + 128 * 4096]
                   .rearrange("(p f) -> p f", p=128))
        for sl in range(8):
            pb = P[1 + (sl % 2)]
            bank_wait(pb)
            if sl == 0:
                wait_io("tensor")
            m = mm(pb[:, 0:512], ones1[0:1, 0:128],
                   w4row[0:1, j * 64 + sl * 8:j * 64 + (sl + 1) * 8]
                   .to_broadcast([1, 8, 64]),
                   True, True, [("s_mm", 1)])["s_mm"]
            wait("vector", "s_mm", m)
            if sl == 0:
                wait_io("vector")
            if j == 0:
                d = op("vector", lambda e, pb=pb, sl=sl: e.tensor_tensor(
                    HH[:, 4096 + sl * 512:4096 + (sl + 1) * 512],
                    AR[:, B_SZQ + sl * 512:B_SZQ + (sl + 1) * 512],
                    pb[:, 0:512], ALU.mult), [("s_dve", 1)])["s_dve"]
            else:
                op("vector", lambda e, pb=pb, sl=sl: e.tensor_tensor(
                    ztmp[:, 0:512],
                    AR[:, B_SZQ + sl * 512:B_SZQ + (sl + 1) * 512],
                    pb[:, 0:512], ALU.mult))
                d = op("vector", lambda e, sl=sl: e.tensor_tensor(
                    HH[:, 4096 + sl * 512:4096 + (sl + 1) * 512],
                    HH[:, 4096 + sl * 512:4096 + (sl + 1) * 512],
                    ztmp[:, 0:512], ALU.add), [("s_dve", 1)])["s_dve"]
            bank_set(pb, "s_dve", d)
        d_prev_e = CNT["s_dve"]

    x2s_free = {0: 0, 1: 0}
    ysb_rd = [0]
    for q in range(NQU):
        for tt in range(2):
            quiesce_io()
            io_x = dma(AR[:, B_XQ:B_XQ + H],
                       x_in[q * TQ + tt * 128:q * TQ + (tt + 1) * 128, :])
            wait_io("vector")
            d_u = stats_and_u(AR[:, B_XQ:B_XQ + H], AR[:, B_XQ:B_XQ + H],
                              128, True, lnw_stream=True)
            wait("tensor", "s_dve", d_u)
            for dc in range(NDC):
                bank_wait(PT)
                m = trans(PT[:, 0:128],
                          AR[:, B_XQ + dc * 128:B_XQ + (dc + 1) * 128],
                          ident, [("s_mm", 1)])["s_mm"]
                wait("vector", "s_mm", m)
                d = op("vector", lambda e, dc=dc, tt=tt: e.tensor_copy(
                    AR[:, B_UT + dc * 260 + tt * 128:
                       B_UT + dc * 260 + (tt + 1) * 128],
                    PT[:, 0:128]), [("s_dve", 1)])["s_dve"]
                bank_set(PT, "s_dve", d)
        d_uT = CNT["s_dve"]

        if DEBUG_TAPS and q == 1:
            wait("gpsimd", "s_dve", CNT["s_dve"])
            dma(dbg_ut.ap(), AR[:, B_UT:B_UT + 8320])
            wait_io("tensor")
        ztmp_free = {0: 0, 1: 0}
        for zb in range(8):
            slot_a = zb % 2
            if aux_free[slot_a]:
                wait("gpsimd", "s_dve", aux_free[slot_a])
            io_mz = dma(auxb[:, slot_a * 512:slot_a * 512 + 512],
                        mupz_in[:, zb * 512:(zb + 1) * 512])
            for dc in range(NDC):
                slot, wcnt = w_load(OFF_Z + zb * 32 + dc)
                wait("tensor", f"s_w{slot}", wcnt)
                if dc == 0:
                    wait("tensor", "s_dve", d_uT)
                    bank_wait(P[0])
                    bank_wait(P[1])
                for tt in range(2):
                    mm(P[tt][:, 0:512],
                       AR[:, B_UT + dc * 260 + tt * 128:
                          B_UT + dc * 260 + (tt + 1) * 128],
                       wb[:, slot * 512:(slot + 1) * 512],
                       start=(dc == 0), stop=(dc == NDC - 1),
                       incs=[("s_mm", 1)])
                w_done(slot)
            m_last = CNT["s_mm"]
            for tt in range(2):
                wait("vector", "s_mm", m_last - 2 + tt + 1)
                if tt == 0:
                    wait_io("vector")
                if ztmp_free[tt]:
                    wait("vector", "s_act", ztmp_free[tt])
                wait("vector", "s_x2", CNT["s_x2"])
                d = op("vector", lambda e, tt=tt, slot_a=slot_a: e.tensor_tensor(
                    ztmp[:, tt * 512:(tt + 1) * 512], P[tt][:, 0:512],
                    auxb[:, slot_a * 512:slot_a * 512 + 512], ALU.mult),
                    [("s_dve", 1)])["s_dve"]
                bank_set(P[tt], "s_dve", d)
                if tt == 1:
                    aux_free[slot_a] = d
                if DEBUG_TAPS and q == 1 and zb == 0 and tt == 0:
                    wait("gpsimd", "s_dve", CNT["s_dve"])
                    dma(dbg_zr.ap(), ztmp[:, 0:512])
                    wait("scalar", "s_io", CNT["s_io"])
                wait("scalar", "s_dve", d)
                ztmp_free[tt] = op("scalar", lambda e, tt=tt, zb=zb: e.activation(
                    out=AR[:, B_SZQ + tt * 4096 + zb * 512:
                           B_SZQ + tt * 4096 + (zb + 1) * 512],
                    in_=ztmp[:, tt * 512:(tt + 1) * 512], func=AF.Silu),
                    [("s_act", 1)])["s_act"]
        a_z = CNT["s_act"]

        if DEBUG_TAPS and q == 1:
            wait("gpsimd", "s_act", CNT["s_act"])
            dma(dbg_sz.ap(), AR[:, B_SZQ:B_SZQ + 8192])
        for tt in range(2):
            ch = q * 2 + tt
            if ysb_rd[0]:
                wait("gpsimd", "s_mm", ysb_rd[0])
            wait("gpsimd", "s_dve", CNT["s_dve"])
            wait("gpsimd", "s_ys", CNT["s_ys"])
            io_y = dma(ysb[:, :], y_loc.ap()[ch * 128:(ch + 1) * 128, :])
            bank_wait(PBC)
            m = mm(PBC[:, 0:64], ones1[0:1, 0:128],
                   offsrow[0:1, ch * 64:(ch + 1) * 64], True, True,
                   [("s_mm", 1)])["s_mm"]
            wait("vector", "s_mm", m)
            d = op("vector", lambda e, ch=ch: e.tensor_tensor(
                wE[:, :], PBC[:, 0:64], comboT[:, ch * 64:(ch + 1) * 64],
                ALU.add), [("s_dve", 1)])["s_dve"]
            bank_set(PBC, "s_dve", d)
            wait("scalar", "s_dve", d)
            a = op("scalar", lambda e: e.activation(
                out=wE[:, :], in_=wE[:, :], func=AF.Exp),
                [("s_act", 1)])["s_act"]
            wait("vector", "s_act", a)
            wait_io("vector")
            if DEBUG_TAPS and q == 1 and tt == 0:
                dma(dbg_y0.ap(), ysb[:, :])
                wait_io("vector")
            for h in range(NH):
                pb = P[2 + (h % 2)]
                bank_wait(pb)
                m = mm(pb[:, 0:64], csave[:, ch * 128:(ch + 1) * 128],
                       HH[:, 4096 + h * 64:4096 + (h + 1) * 64], True, True,
                       [("s_mm", 1)])["s_mm"]
                wait("vector", "s_mm", m)
                op("vector", lambda e, pb=pb, h=h: e.tensor_scalar(
                    t64[:, :], pb[:, 0:64], wE[:, h:h + 1], None, ALU.mult))
                d = op("vector", lambda e, h=h: e.tensor_tensor(
                    ysb[:, h * 64:(h + 1) * 64], ysb[:, h * 64:(h + 1) * 64],
                    t64[:, :], ALU.add), [("s_dve", 1)])["s_dve"]
                bank_set(pb, "s_dve", d)
            if DEBUG_TAPS and q == 1 and tt == 0:
                wait("gpsimd", "s_dve", CNT["s_dve"])
                dma(dbg_y1.ap(), ysb[:, :])
                wait_io("vector")
            wait("vector", "s_act", a_z)
            op("vector", lambda e, tt=tt: e.tensor_tensor(
                ysb[:, :], ysb[:, :],
                AR[:, B_SZQ + tt * 4096:B_SZQ + (tt + 1) * 4096], ALU.mult))
            if DEBUG_TAPS and q == 1 and tt == 0:
                wait("gpsimd", "s_dve", CNT["s_dve"])
                dma(dbg_y2.ap(), ysb[:, :])
                wait_io("vector")
            for k in range(8):
                op("vector", lambda e, k=k: e.bn_stats(
                    stat[:, 6 * k:6 * k + 6], ysb[:, 512 * k:512 * (k + 1)]),
                    [("s_dve", 1)])
            wait("vector", "s_dve", CNT["s_dve"])
            op("vector", lambda e: e.bn_aggr(stat[:, 48:50], stat[:, 0:48]),
               [("s_dve", 1)])
            wait("vector", "s_dve", CNT["s_dve"])
            op("vector", lambda e: e.scalar_tensor_tensor(
                t0v[:, :], stat[:, 48:49], stat[:, 48:49], stat[:, 49:50],
                ALU.mult, ALU.add))
            d = op("vector", lambda e: e.tensor_scalar(
                t0v[:, :], t0v[:, :], EPS, None, ALU.add),
                [("s_dve", 1)])["s_dve"]
            wait("scalar", "s_dve", d)
            a = op("scalar", lambda e: e.activation(
                out=t1v[:, :], in_=t0v[:, :], func=AF.Ln),
                [("s_act", 1)])["s_act"]
            wait("scalar", "s_act", a)
            a = op("scalar", lambda e: e.activation(
                out=gval[:, :], in_=t1v[:, :], func=AF.Exp, scale=-0.5),
                [("s_act", 1)])["s_act"]
            wait("vector", "s_act", a)
            op("vector", lambda e: e.tensor_scalar(
                ysb[:, :], ysb[:, :], gval[:, :], None, ALU.mult))
            for sl in range(8):
                slot_a = sl % 2
                if aux_free[slot_a]:
                    wait("gpsimd", "s_dve", aux_free[slot_a])
                io_nw = dma(auxb[:, slot_a * 512:slot_a * 512 + 512],
                            nrmw_in[:, sl * 512:(sl + 1) * 512])
                wait_io("vector")
                d = op("vector", lambda e, sl=sl, slot_a=slot_a: e.tensor_tensor(
                    ysb[:, sl * 512:(sl + 1) * 512],
                    ysb[:, sl * 512:(sl + 1) * 512],
                    auxb[:, slot_a * 512:slot_a * 512 + 512], ALU.mult),
                    [("s_dve", 1)])["s_dve"]
                aux_free[slot_a] = d
            d_yh = CNT["s_dve"]
            if DEBUG_TAPS and q == 1 and tt == 0:
                wait("gpsimd", "s_dve", CNT["s_dve"])
                dma(dbg_yh.ap(), ysb[:, :])
                wait_io("vector")
            wait("tensor", "s_dve", d_yh)
            last_m = 0
            for dc in range(NDC):
                bank_wait(PT)
                m = trans(PT[:, 0:128], ysb[:, dc * 128:(dc + 1) * 128],
                          ident, [("s_mm", 1)])["s_mm"]
                last_m = m
                wait("vector", "s_mm", m)
                d = op("vector", lambda e, dc=dc, tt=tt: e.tensor_copy(
                    AR[:, B_YHT + dc * 256 + tt * 128:
                       B_YHT + dc * 256 + (tt + 1) * 128],
                    PT[:, 0:128]), [("s_dve", 1)])["s_dve"]
                bank_set(PT, "s_dve", d)
            ysb_rd[0] = last_m
        d_yhT = CNT["s_dve"]

        for ob in range(8):
            for fc in range(NDC):
                slot, wcnt = w_load(OFF_O + ob * 32 + fc)
                wait("tensor", f"s_w{slot}", wcnt)
                if fc == 0:
                    wait("tensor", "s_dve", d_yhT)
                    bank_wait(P[4])
                    bank_wait(P[5])
                for tt in range(2):
                    mm(P[4 + tt][:, 0:512],
                       AR[:, B_YHT + fc * 256 + tt * 128:
                          B_YHT + fc * 256 + (tt + 1) * 128],
                       wb[:, slot * 512:(slot + 1) * 512],
                       start=(fc == 0), stop=(fc == NDC - 1),
                       incs=[("s_mm", 1)])
                w_done(slot)
            m_last = CNT["s_mm"]
            for tt in range(2):
                if x2s_free[tt]:
                    wait("gpsimd", "s_dve", x2s_free[tt])
                wait("gpsimd", "s_x2", CNT["s_x2"])
                io_xs = dma(x2s[:, tt * 512:(tt + 1) * 512],
                            x_in[q * TQ + tt * 128:q * TQ + (tt + 1) * 128,
                                 ob * 512:(ob + 1) * 512])
                wait("vector", "s_mm", m_last - 2 + tt + 1)
                wait_io("vector")
                d = op("vector", lambda e, tt=tt: e.tensor_tensor(
                    x2s[:, tt * 512:(tt + 1) * 512], P[4 + tt][:, 0:512],
                    x2s[:, tt * 512:(tt + 1) * 512], ALU.add),
                    [("s_dve", 1)])["s_dve"]
                bank_set(P[4 + tt], "s_dve", d)
                x2s_free[tt] = d
                wait("gpsimd", "s_dve", d)
                dma(x2_loc.ap()[q * TQ + tt * 128:q * TQ + (tt + 1) * 128,
                                ob * 512:(ob + 1) * 512],
                    x2s[:, tt * 512:(tt + 1) * 512], sem="s_x2")

    for st in range(2):
        for tt in range(4):
            quiesce_io()
            wait("gpsimd", "s_x2", CNT["s_x2"])
            wait("gpsimd", "s_st", CNT["s_st"])
            io_x = dma(ysb[:, :], x2_loc.ap()[st * 512 + tt * 128:
                                              st * 512 + (tt + 1) * 128, :])
            wait_io("vector")
            d_u = stats_and_u(ysb[:, :], ysb[:, :], 128, False)
            wait("tensor", "s_dve", d_u)
            for dc in range(NDC):
                bank_wait(PT)
                m = trans(PT[:, 0:128], ysb[:, dc * 128:(dc + 1) * 128],
                          ident, [("s_mm", 1)])["s_mm"]
                wait("vector", "s_mm", m)
                d = op("vector", lambda e, dc=dc, tt=tt: e.tensor_copy(
                    AR[:, F_U2T + dc * 512 + tt * 128:
                       F_U2T + dc * 512 + (tt + 1) * 128],
                    PT[:, 0:128]), [("s_dve", 1)])["s_dve"]
                bank_set(PT, "s_dve", d)
        d_u2T = CNT["s_dve"]

        for fb in range(8):
            for fg in range(4):
                for dc in range(NDC):
                    slot, wcnt = w_load(OFF_F1 + fb * 128 + fg * 32 + dc)
                    wait("tensor", f"s_w{slot}", wcnt)
                    if dc == 0:
                        wait("tensor", "s_dve", d_u2T)
                        for j in range(4):
                            bank_wait(P[j])
                    for j in range(4):
                        mm(P[j][:, 0:512],
                           wb[:, slot * 512 + j * 128:slot * 512 + (j + 1) * 128],
                           AR[:, F_U2T + dc * 512:F_U2T + (dc + 1) * 512],
                           start=(dc == 0), stop=(dc == NDC - 1),
                           incs=[("s_mm", 1)])
                    w_done(slot)
                m_last = CNT["s_mm"]
                for j in range(4):
                    fc = fb * 16 + fg * 4 + j
                    wait("scalar", "s_mm", m_last - 4 + j + 1)
                    a = op("scalar", lambda e, j=j, fc=fc, fg=fg: e.activation(
                        out=HH[:, (fg * 4 + j) * 512:(fg * 4 + j + 1) * 512],
                        in_=P[j][:, 0:512], func=AF.Gelu_apprx_tanh,
                        bias=cst[:, C_B1 + fc:C_B1 + fc + 1]),
                        [("s_act", 1)])["s_act"]
                    bank_set(P[j], "s_act", a)
            a_h = CNT["s_act"]
            for og in range(8):
                for fc16 in range(16):
                    slot, wcnt = w_load(OFF_F2 + fb * 128 + og * 16 + fc16)
                    wait("tensor", f"s_w{slot}", wcnt)
                    if fc16 == 0:
                        if og == 0:
                            wait("tensor", "s_act", a_h)
                        for j in range(4):
                            bank_wait(P[j])
                    for j in range(4):
                        mm(P[j][:, 0:512],
                           wb[:, slot * 512 + j * 128:slot * 512 + (j + 1) * 128],
                           HH[:, fc16 * 512:(fc16 + 1) * 512],
                           start=(fc16 == 0), stop=(fc16 == 15),
                           incs=[("s_mm", 1)])
                    w_done(slot)
                m_last = CNT["s_mm"]
                for j in range(4):
                    oc = og * 4 + j
                    wait("vector", "s_mm", m_last - 4 + j + 1)
                    if fb == 0:
                        d = op("vector", lambda e, j=j, oc=oc: e.tensor_scalar(
                            AR[:, F_O2FM + oc * 512:F_O2FM + (oc + 1) * 512],
                            P[j][:, 0:512], cst[:, C_B2 + oc:C_B2 + oc + 1],
                            None, ALU.add), [("s_dve", 1)])["s_dve"]
                    else:
                        d = op("vector", lambda e, j=j, oc=oc: e.tensor_tensor(
                            AR[:, F_O2FM + oc * 512:F_O2FM + (oc + 1) * 512],
                            AR[:, F_O2FM + oc * 512:F_O2FM + (oc + 1) * 512],
                            P[j][:, 0:512], ALU.add), [("s_dve", 1)])["s_dve"]
                    bank_set(P[j], "s_dve", d)
        d_o2 = CNT["s_dve"]

        for tt in range(4):
            quiesce_io()
            wait("gpsimd", "s_x2", CNT["s_x2"])
            wait("gpsimd", "s_st", CNT["s_st"])
            io_x = dma(ysb[:, :], x2_loc.ap()[st * 512 + tt * 128:
                                              st * 512 + (tt + 1) * 128, :])
            wait("tensor", "s_dve", d_o2)
            wait_io("vector")
            for oc in range(NDC):
                bank_wait(PT)
                m = trans(PT[:, 0:128],
                          AR[:, F_O2FM + oc * 512 + tt * 128:
                             F_O2FM + oc * 512 + (tt + 1) * 128],
                          ident, [("s_mm", 1)])["s_mm"]
                wait("vector", "s_mm", m)
                d = op("vector", lambda e, oc=oc: e.tensor_tensor(
                    ysb[:, oc * 128:(oc + 1) * 128],
                    ysb[:, oc * 128:(oc + 1) * 128],
                    PT[:, 0:128], ALU.add), [("s_dve", 1)])["s_dve"]
                bank_set(PT, "s_dve", d)
            wait("gpsimd", "s_dve", CNT["s_dve"])
            op("gpsimd", lambda e, st=st, tt=tt: e.dma_start(
                out=out_t[st * 512 + tt * 128:st * 512 + (tt + 1) * 128, :],
                in_=ysb[:, :]), [("s_st", 16)])

    wait("gpsimd", "s_st", CNT["s_st"])

    with (
        nc.semaphore("s_w0") as _sw0,
        nc.semaphore("s_w1") as _sw1,
        nc.semaphore("s_w2") as _sw2,
        nc.semaphore("s_io") as _sio,
        nc.semaphore("s_mm") as _smm,
        nc.semaphore("s_dve") as _sdve,
        nc.semaphore("s_act") as _sact,
        nc.semaphore("s_cc") as _scc,
        nc.semaphore("s_st") as _sst,
        nc.semaphore("s_ys") as _sys,
        nc.semaphore("s_x2") as _sx2,
        nc.Block() as block,
    ):
        sems.update({"s_w0": _sw0, "s_w1": _sw1, "s_w2": _sw2,
                     "s_io": _sio, "s_mm": _smm,
                     "s_dve": _sdve, "s_act": _sact, "s_cc": _scc,
                     "s_st": _sst, "s_ys": _sys, "s_x2": _sx2})

        @block.sync
        def _(sync):
            for fn in OPS["sync"]:
                fn(sync)

        @block.gpsimd
        def _(gpsimd):
            for fn in OPS["gpsimd"]:
                fn(gpsimd)

        @block.tensor
        def _(tensor):
            for fn in OPS["tensor"]:
                fn(tensor)

        @block.vector
        def _(vector):
            for fn in OPS["vector"]:
                fn(vector)

        @block.scalar
        def _(scalar):
            for fn in OPS["scalar"]:
                fn(scalar)

    for t in reversed(ctx):
        t.__exit__(None, None, None)
    return nc


def kernel(**inputs):
    x = np.ascontiguousarray(np.asarray(inputs["L0_attn_hidden_states"],
                                        dtype=np.float32))
    ln_w = np.asarray(inputs["L0_attn_ln_w"], np.float32)
    W_in = np.asarray(inputs["L0_attn_in_proj_w"], np.float32)
    conv_w = np.asarray(inputs["L0_attn_conv_w"], np.float32)
    A_log = np.asarray(inputs["L0_attn_A_log"], np.float32)
    dt_bias = np.asarray(inputs["L0_attn_dt_bias"], np.float32)
    D = np.asarray(inputs["L0_attn_D"], np.float32)
    norm_w = np.asarray(inputs["L0_attn_norm_w"], np.float32)
    W_out = np.asarray(inputs["L0_attn_out_proj_w"], np.float32)
    mup = np.asarray(inputs["L0_attn_mup_vector"], np.float32)
    W1 = np.asarray(inputs["L0_ffn_inter_w"], np.float32)
    b1 = np.asarray(inputs["L0_ffn_inter_b"], np.float32)
    W2 = np.asarray(inputs["L0_ffn_output_w"], np.float32)
    b2 = np.asarray(inputs["L0_ffn_output_b"], np.float32)

    wflat = _host_weight_stream(W_in, W_out, W1, W2)
    part_elems = (NBLK // 4) * BLK
    shard = part_elems // 8
    xr = x.reshape(2, 4096, H)

    mup_pad = np.zeros(67 * 128, np.float32)
    mup_pad[:8512] = mup
    conv_pad = np.zeros((35 * 128, 4), np.float32)
    conv_pad[:CONV] = conv_w
    cst = np.zeros((128, CST_W), np.float32)
    cst[:, C_ID:C_ID + 128] = np.eye(128, dtype=np.float32)
    cst[:, C_LT:C_LT + 128] = np.triu(np.ones((128, 128), np.float32))
    cst[:, C_CONV:C_CONV + 140] = conv_pad.reshape(35, 128, 4).transpose(
        1, 0, 2).reshape(128, 140)
    cst[:, C_MUP:C_MUP + 67] = mup_pad.reshape(67, 128).T
    cst[:, C_DBC:C_DBC + 64] = np.broadcast_to(D, (128, 64))
    cst[:, C_B1:C_B1 + 128] = b1.reshape(128, 128).T
    cst[:, C_B2:C_B2 + 32] = b2.reshape(32, 128).T
    cst[0:64, C_AL] = A_log
    cst[0:64, C_DTB] = dt_bias

    lnw_bc = np.ascontiguousarray(np.broadcast_to(ln_w, (128, H)))
    mupz_bc = np.ascontiguousarray(np.broadcast_to(mup[:H], (128, H)))
    normw_bc = np.ascontiguousarray(np.broadcast_to(norm_w, (128, H)))

    in_maps = []
    for core in range(8):
        b, qq = divmod(core, 4)
        xq = np.ascontiguousarray(xr[b, qq * R:(qq + 1) * R])
        if qq > 0:
            halo = np.ascontiguousarray(xr[b, qq * R - 4:qq * R])
        else:
            halo = np.zeros((4, H), np.float32)
        cstc = cst.copy()
        sel = np.zeros((4, 4), np.float32)
        mcol = np.zeros((4,), np.float32)
        for j in range(4):
            if j < qq:
                mcol[j] = 1.0
                for i in range(j + 1, qq):
                    sel[j, i] = 1.0
        cstc[0:4, C_SEL:C_SEL + 4] = sel.T
        cstc[0:4, C_M] = mcol
        im = {
            "x": xq,
            "halo": halo,
            "lnw_bc": lnw_bc,
            "mupz_bc": mupz_bc,
            "normw_bc": normw_bc,
            "cst_in": cstc,
        }
        for p in range(4):
            base = p * part_elems
            im[f"wsh{p}"] = np.ascontiguousarray(
                wflat[base + core * shard:base + (core + 1) * shard])
        in_maps.append(im)

    nc = build_program()
    res = run_bass_kernel_spmd(nc, in_maps, core_ids=list(range(8)))
    out = np.empty((2, 4096, H), np.float32)
    for core in range(8):
        b, qq = divmod(core, 4)
        out[b, qq * R:(qq + 1) * R] = res.results[core]["out"]
    return out
